# version 03
# baseline (speedup 1.0000x reference)
import math
import numpy as np
import concourse.bass as bass
import concourse.mybir as mybir
from concourse.bass_utils import run_bass_kernel_spmd

F32 = mybir.dt.float32
BF16 = mybir.dt.bfloat16
I32 = mybir.dt.int32
ALU = mybir.AluOpType
ACTF = mybir.ActivationFunctionType

T = 2048
E = 2048
NH = 16
HD = 128
NE = E // 128
INTER = 5504
NI = INTER // 128
FFT_N = 4096
NEG_MIN = float(np.finfo(np.float32).min)

ENGS = ("pe", "dve", "act", "gps", "sp")


class Emitter:

    def __init__(self):
        self.q = {e: [] for e in ENGS}
        self.cnt = {}
        self.sems = {}

    def sem(self, name):
        if name not in self.cnt:
            self.cnt[name] = 0
        return name

    def on(self, eng, fn):
        self.q[eng].append(fn)

    def wait(self, eng, sem, val):
        if val <= 0:
            return
        assert val <= self.cnt.get(sem, 0), f"wait {sem}>={val} but only {self.cnt.get(sem,0)} emitted"
        self.on(eng, lambda E, s=sem, v=val: E.wait_ge(self.sems[s], v))

    def step(self, eng, fn, inc=None, inc_by=1, pad=True):
        if inc is None:
            self.on(eng, fn)
            return None
        self.sem(inc)
        self.cnt[inc] += inc_by
        if pad and eng in ("dve", "act") and getattr(self, "_g", None) is not None:
            self.on(eng, fn)
            self.gap(eng, self._g)

            def gapinc(E, s=inc, b=inc_by, g=self._g):
                src = g.sb["pool"][:, 16380:16381]
                dst = g.sb["pool"][:, 16381:16382]
                if eng == "act":
                    ins = E.activation(dst, src, ACTF.Identity)
                else:
                    ins = E.tensor_copy(out=dst, in_=src)
                ins.then_inc(self.sems[s], b)
                return ins

            self.on(eng, gapinc)
            return self.cnt[inc]

        def wrapped(E, f=fn, s=inc, b=inc_by):
            ins = f(E)
            ins.then_inc(self.sems[s], b)
            return ins

        self.on(eng, wrapped)
        return self.cnt[inc]

    def dma(self, eng, out, in_, inc, slow=False, delay=False):
        if delay or eng == "act":
            self.on(eng, lambda E: E.nop(cycle_cnt=3000))
        if slow:
            return self.step(eng, lambda E, o=out, i=in_:
                             E.dma_start(out=o, in_=i, allow_slow_non_contiguous=True),
                             inc=inc, inc_by=16, pad=False)
        return self.step(eng, lambda E, o=out, i=in_: E.dma_start(out=o, in_=i), inc=inc, inc_by=16,
                         pad=False)

    def gap(self, eng, g, inc=None):
        src = g.sb["pool"][:, 16380:16381]
        dst = g.sb["pool"][:, 16381:16382]
        if eng == "act":
            return self.step(eng, lambda E, o=dst, i=src: E.activation(o, i, ACTF.Identity), inc=inc)
        return self.step(eng, lambda E, o=dst, i=src: E.tensor_copy(out=o, in_=i), inc=inc)

    def barrier(self, engines=("pe", "dve", "act", "sp")):
        vals = {}
        for e in engines:
            self.mark(e, f"bar_{e}") if False else None
        for e in engines:
            self.step(e, lambda E: E.drain(), inc=f"bar_{e}", pad=False)
            vals[e] = self.cnt[f"bar_{e}"]
        for e in engines:
            for o in engines:
                if o != e:
                    self.wait(e, f"bar_{o}", vals[o])

    def mark(self, eng, sem):
        if eng in ("dve", "act") and getattr(self, "_g", None) is not None:
            self.gap(eng, self._g)
            self.gap(eng, self._g)
        return self.step(eng, lambda E: E.drain(), inc=sem)


class Psum:

    def __init__(self, em, tensors):
        self.em = em
        self.t = tensors
        self.free_cnt = [0] * len(tensors)
        self.use_cnt = [0] * len(tensors)
        self.sem = [em.sem(f"psf{i}") for i in range(len(tensors))]
        self.rr = 0

    def acquire(self, eng="pe", bank=None):
        if bank is None:
            bank = self.rr
            self.rr = (self.rr + 1) % len(self.t)
        self.em.wait(eng, self.sem[bank], self.use_cnt[bank])
        self.use_cnt[bank] += 1
        return bank

    def release(self, bank, eng, fn):
        self.em.step(eng, fn, inc=self.sem[bank])
        self.free_cnt[bank] += 1


def _split64(x):
    hi = np.asarray(x, dtype=np.float32)
    lo = np.asarray(x - hi.astype(np.float64), dtype=np.float32)
    return hi, lo


def host_constants():
    c = {}
    i = np.arange(64, dtype=np.float64)
    inv_freq = 1.0 / (10000.0 ** (2.0 * i / HD))
    t = np.arange(T, dtype=np.float64)
    ang = t[None, :] * inv_freq[:, None]
    sc = 1.0 / math.sqrt(float(HD))
    for nm, arr in (("qcos", np.cos(ang) * sc), ("qsin", np.sin(ang) * sc),
                    ("kcos", np.cos(ang)), ("ksin", np.sin(ang))):
        t3 = arr.reshape(64, 16, 128).transpose(2, 1, 0).reshape(128, 1024)
        hi, lo = _split64(t3)
        c[nm + "_hi"], c[nm + "_lo"] = hi, lo
    tri = np.tril(np.ones((128, 128), dtype=np.float32))
    c["m01"] = tri
    c["mneg"] = (1.0 - tri) * NEG_MIN
    c["id128"] = np.eye(128, dtype=np.float32)
    c["ones128"] = np.ones((128, 1), dtype=np.float32)
    c["ones1x128"] = np.ones((1, 128), dtype=np.float32)
    a = np.arange(64, dtype=np.float64)
    r = np.arange(128, dtype=np.float64)
    b = np.arange(32, dtype=np.float64)
    th1 = -2.0 * np.pi * (32.0 * a[None, :, None] + b[:, None, None]) * r[None, None, :] / FFT_N
    f1re = np.cos(th1).transpose(1, 0, 2).reshape(64, 32 * 128)
    f1im = np.sin(th1).transpose(1, 0, 2).reshape(64, 32 * 128)
    c["f1re_hi"], c["f1re_lo"] = _split64(f1re)
    c["f1im_hi"], c["f1im_lo"] = _split64(f1im)
    q_ = np.arange(32, dtype=np.float64)
    f2 = np.exp(-2j * np.pi * np.outer(b, q_) / 32.0)
    blk = np.zeros((128, 128), dtype=np.complex128)
    for ch in range(4):
        blk[ch * 32:(ch + 1) * 32, ch * 32:(ch + 1) * 32] = f2
    c["f2re_hi"], c["f2re_lo"] = _split64(blk.real)
    c["f2im_hi"], c["f2im_lo"] = _split64(blk.imag)
    c["f2imn_hi"], c["f2imn_lo"] = _split64(-blk.imag)
    w32i = np.exp(+2j * np.pi * np.outer(q_, q_) / 32.0)
    blki = np.zeros((128, 128), dtype=np.complex128)
    for ch in range(4):
        blki[ch * 32:(ch + 1) * 32, ch * 32:(ch + 1) * 32] = w32i
    c["are_hi"], c["are_lo"] = _split64(blki.real)
    c["aim_hi"], c["aim_lo"] = _split64(blki.imag)
    c["aimn_hi"], c["aimn_lo"] = _split64(-blki.imag)
    beta = np.tile(np.arange(32, dtype=np.float64), 4)
    thw = 2.0 * np.pi * beta[:, None] * r[None, :] / FFT_N
    c["twre_hi"], c["twre_lo"] = _split64(np.cos(thw) / FFT_N)
    c["twim_hi"], c["twim_lo"] = _split64(np.sin(thw) / FFT_N)
    al = np.arange(64, dtype=np.float64)
    thb = 2.0 * np.pi * np.outer(r, al) / 128.0
    c["bcos_hi"], c["bcos_lo"] = _split64(np.cos(thb))
    c["bnsin_hi"], c["bnsin_lo"] = _split64(-np.sin(thb))
    return c


def host_prep(inputs):
    c = host_constants()
    d = {}
    x = np.ascontiguousarray(inputs["L0_attn_a_x"], dtype=np.float32)
    wqkv = np.asarray(inputs["L0_attn_a_W_qkv"], dtype=np.float32)
    perm = np.arange(6144)
    for h in range(NH):
        for base in (0, 2048):
            blk = base + h * 128 + np.arange(128)
            perm[base + h * 128: base + h * 128 + 128] = np.concatenate([blk[0::2], blk[1::2]])
    wqkvT = np.ascontiguousarray(wqkv[perm].T)
    d["wqkvT"] = wqkvT
    d["wprojT"] = np.ascontiguousarray(np.asarray(inputs["L0_attn_a_W_proj"], dtype=np.float32).T)
    d["lut"] = np.ascontiguousarray(np.asarray(inputs["L0_ffn_lut"], dtype=np.float32).reshape(1, 16))
    d["gidxT"] = np.ascontiguousarray(np.asarray(inputs["L0_ffn_gate_idx"], dtype=np.int32).T)
    d["uidxT"] = np.ascontiguousarray(np.asarray(inputs["L0_ffn_up_idx"], dtype=np.int32).T)
    d["didxT"] = np.ascontiguousarray(np.asarray(inputs["L0_ffn_down_idx"], dtype=np.int32).T)
    for nm, key in (("wqT", "L0_attn_b_Wq"), ("wkT", "L0_attn_b_Wk"), ("wvT", "L0_attn_b_Wv"),
                    ("woT", "L0_attn_b_Wo")):
        d[nm] = np.ascontiguousarray(np.asarray(inputs[key], dtype=np.float32).T)
    for nm, key in (("bq", "L0_attn_b_bq"), ("bk", "L0_attn_b_bk"), ("bv", "L0_attn_b_bv"),
                    ("bo", "L0_attn_b_bo"), ("dskip", "L0_attn_b_D_skip"),
                    ("cqb", "L0_attn_b_convq_b"), ("ckb", "L0_attn_b_convk_b")):
        d[nm] = np.ascontiguousarray(np.asarray(inputs[key], dtype=np.float32).reshape(E, 1))
    d["cqw"] = np.ascontiguousarray(np.asarray(inputs["L0_attn_b_convq_w"], dtype=np.float32).reshape(E, 3))
    d["ckw"] = np.ascontiguousarray(np.asarray(inputs["L0_attn_b_convk_w"], dtype=np.float32).reshape(E, 3))
    ker = np.asarray(inputs["L0_attn_b_ssm_kernel"], dtype=np.float32)
    kr = ker.reshape(NE, 128, 64, 32)
    kr = kr.transpose(2, 0, 3, 1).reshape(64, NE * 32 * 128)
    d["ker_s1"] = np.ascontiguousarray(kr)
    d.update(c)
    in_maps = [dict(d, x=np.ascontiguousarray(x[core])) for core in range(8)]
    return in_maps



class Ctx:
    pass


def declare_io(nc, taps):
    g = Ctx()
    g.taps = dict(taps)
    dt = {}
    def din(name, shape, dtype=F32):
        dt[name] = nc.dram_tensor(name, list(shape), dtype, kind="ExternalInput").ap()
    din("x", (T, E))
    din("wqkvT", (E, 3 * E)); din("wprojT", (E, E))
    din("lut", (1, 16))
    din("gidxT", (E, INTER), I32); din("uidxT", (E, INTER), I32); din("didxT", (INTER, E), I32)
    for nm in ("wqT", "wkT", "wvT", "woT"):
        din(nm, (E, E))
    for nm in ("bq", "bk", "bv", "bo", "dskip", "cqb", "ckb"):
        din(nm, (E, 1))
    din("cqw", (E, 3)); din("ckw", (E, 3))
    din("ker_s1", (64, NE * 32 * 128))
    for nm in ("qcos", "qsin", "kcos", "ksin"):
        din(nm + "_hi", (128, 1024)); din(nm + "_lo", (128, 1024))
    din("m01", (128, 128)); din("mneg", (128, 128)); din("id128", (128, 128))
    din("ones128", (128, 1)); din("ones1x128", (1, 128))
    din("f1re_hi", (64, 4096)); din("f1re_lo", (64, 4096))
    din("f1im_hi", (64, 4096)); din("f1im_lo", (64, 4096))
    for nm in ("f2re", "f2im", "f2imn", "are", "aim", "aimn", "twre", "twim"):
        din(nm + "_hi", (128, 128)); din(nm + "_lo", (128, 128))
    din("bcos_hi", (128, 64)); din("bcos_lo", (128, 64))
    din("bnsin_hi", (128, 64)); din("bnsin_lo", (128, 64))
    g.d = dt
    g.out = nc.dram_tensor("out", [T, E], F32, kind="ExternalOutput").ap()
    g.tap_t = {}
    for nm, shape in g.taps.items():
        g.tap_t[nm] = nc.dram_tensor("tap_" + nm, list(shape), F32, kind="ExternalOutput").ap()
    g.s_xres = nc.dram_tensor("s_xres", [NE, 128, T], F32, kind="Internal").ap()
    g.s_q = nc.dram_tensor("s_q", [NH, 16, 128, 128], F32, kind="Internal").ap()
    g.s_k = nc.dram_tensor("s_k", [NH, 16, 128, 128], F32, kind="Internal").ap()
    g.s_vt = nc.dram_tensor("s_vt", [NH, 16, 128, 128], F32, kind="Internal").ap()
    g.s_wg = nc.dram_tensor("s_wg", [NE, 128, INTER], F32, kind="Internal").ap()
    g.s_wu = nc.dram_tensor("s_wu", [NE, 128, INTER], F32, kind="Internal").ap()
    g.s_wd = nc.dram_tensor("s_wd", [NI, 128, E], F32, kind="Internal").ap()
    g.s_h = nc.dram_tensor("s_h", [NI, 128, T], F32, kind="Internal").ap()
    g.s_qc = nc.dram_tensor("s_qc", [NE, 128, T], F32, kind="Internal").ap()
    g.s_kv = nc.dram_tensor("s_kv", [NE, 128, T], F32, kind="Internal").ap()
    g.s_ks = nc.dram_tensor("s_ks", [NE, 2, 128, FFT_N], F32, kind="Internal").ap()
    g.s_yg = nc.dram_tensor("s_yg", [NE, 128, T], F32, kind="Internal").ap()
    g.s_x2 = nc.dram_tensor("s_x2", [NE, 128, T], F32, kind="Internal").ap()
    return g


def build_program(taps=(), stop_after=None):
    nc = bass.Bass("TRN2", target_bir_lowering=False)
    g = declare_io(nc, taps)
    em = Emitter()
    g.em = em
    em._g = g
    import contextlib
    with contextlib.ExitStack() as st:
        sb = {}
        def sbuf(name, shape, dtype=F32):
            sb[name] = st.enter_context(nc.sbuf_tensor(name, list(shape), dtype))
            return sb[name]
        g.sb = sb
        sbuf("xres", (128, NE * T))
        sbuf("pool", (128, 16 * 1024))
        sbuf("small", (128, 512))
        ps = [st.enter_context(nc.psum_tensor(f"ps{i}", [128, 512], F32)) for i in range(8)]
        g.ps = Psum(em, [p[:] for p in ps])
        g.pst = ps

        phases(g, stop_after)

        sem_names = [n for n in em.cnt.keys()]
        sems = {}
        for n in sem_names:
            sems[n] = st.enter_context(nc.semaphore(n))
        em.sems = sems
        with nc.Block() as block:
            for eng, sec in (("pe", block.tensor), ("dve", block.vector),
                             ("act", block.scalar), ("gps", block.gpsimd),
                             ("sp", block.sync)):
                if em.q[eng]:
                    def body(E, ops=tuple(em.q[eng])):
                        for f in ops:
                            f(E)
                    sec(body)
    return nc, g



def phases(g, stop_after):
    em = g.em
    p_load_consts(g)
    p_load_x(g)
    if stop_after == "A0":
        return finish(g)
    em.barrier()
    p_ln(g, "ln1", spill=True)
    if stop_after == "LN1":
        return finish(g)
    em.barrier()
    p_qkv(g)
    if stop_after == "A2":
        return finish(g)
    em.barrier()
    p_attn(g)
    if stop_after == "A3":
        return finish(g)
    em.barrier()
    p_ln(g, "ln2", spill=True, ready=g.a3_done)
    em.barrier()
    p_ffn_build(g)
    em.barrier()
    p_ffn(g, "f1", out_mode="xres")
    g.a3_done = g.ffn_done
    if stop_after == "F1":
        return finish(g)
    em.barrier()
    p_ln(g, "ln3", spill=True, ready=g.ffn_done)
    if stop_after == "LN3":
        g.a3_done = g.ln_done
        return finish(g)
    em.barrier()
    p_ssm_proj(g)
    em.barrier()
    p_ssm_fft(g)
    em.barrier()
    p_ssm_out(g)
    g.a3_done = g.ss_out_done
    if stop_after == "B":
        return finish(g)
    em.barrier()
    p_ln(g, "ln4", spill=True, ready=g.ss_out_done)
    em.barrier()
    p_ffn(g, "f2", out_mode="final")
    finish(g)


def pc(g, off, n):
    return g.sb["pool"][:, off:off + n]


ID_OFF = 15360


def small_col(g, chunk, col):
    return g.sb["small"][:, chunk * 16 + col: chunk * 16 + col + 1]


def p_load_consts(g):
    em, sb, d = g.em, g.sb, g.d
    small = sb["small"]
    em.dma("sp", small[:, 0:16 * NE].rearrange("p (c w) -> p c w", w=16)[:, :, 0:3],
           d["cqw"].rearrange("(c p) w -> p c w", p=128), inc="spq", slow=True)
    em.dma("sp", small[:, 0:16 * NE].rearrange("p (c w) -> p c w", w=16)[:, :, 3:6],
           d["ckw"].rearrange("(c p) w -> p c w", p=128), inc="spq", slow=True)
    for i, nm in enumerate(("bq", "bk", "bv", "bo", "dskip", "cqb", "ckb")):
        em.dma("sp", small[:, 0:16 * NE].rearrange("p (c w) -> p c w", w=16)[:, :, 6 + i:7 + i],
               d[nm].rearrange("(c p) w -> p c w", p=128), inc="spq", slow=True)
    g.small_ready = em.cnt["spq"]


def p_load_x(g):
    em, sb, d = g.em, g.sb, g.d
    xres = sb["xres"][:]
    work = pc(g, 0, 2 * T)
    cid = pc(g, ID_OFF, 128)
    em.dma("sp", cid, d["id128"][:, :], inc="spq")
    em.wait("pe", "spq", em.cnt["spq"])
    for tb in range(16):
        buf = work[:, (tb % 2) * T:(tb % 2) * T + T]
        if tb >= 2:
            em.wait("sp", "xld_pe", em.cnt["xld_pe"])
        lv = em.dma("sp", buf, d["x"][tb * 128:(tb + 1) * 128, :], inc="spq")
        em.wait("pe", "spq", lv)
        for ec in range(16):
            bank = g.ps.acquire("pe")
            pst = g.pst[bank][:, 0:128]
            tv = em.step("pe", lambda En, o=pst, i=buf[:, ec * 128:(ec + 1) * 128], ident=cid:
                         En.transpose(o, i, ident), inc="peq")
            em.wait("dve", "peq", tv)
            dst = xres[:, ec * T + tb * 128: ec * T + tb * 128 + 128]
            g.ps.release(bank, "dve", lambda En, o=dst, i=pst: En.tensor_copy(out=o, in_=i))
        em.mark("pe", "xld_pe")
    em.mark("dve", "dvm")
    g.xres_ready = em.cnt["dvm"]


def p_ln(g, tag, spill, ready=None):
    em, sb, d = g.em, g.sb, g.d
    xres = sb["xres"][:]
    ones = pc(g, ID_OFF + 128, 1)
    if not getattr(g, "_ones_loaded", False):
        em.dma("sp", ones, d["ones128"][:, :], inc="spq")
        em.dma("sp", g.sb["pool"][0:1, ID_OFF + 130:ID_OFF + 258], d["ones1x128"][:, :], inc="spq")
        g._ones_loaded = True
    ones1x = g.sb["pool"][0:1, ID_OFF + 130:ID_OFF + 258]
    em.wait("pe", "spq", em.cnt["spq"])
    if ready is None:
        ready = ("dvm", g.xres_ready)
    em.wait("act", ready[0], ready[1])
    em.wait("act", ready[0], ready[1])
    sqpe_base = g.em.cnt.get("ln_sqpe", 0)
    banks_s = [g.ps.acquire("pe", bank=i) for i in range(4)]
    banks_q = [g.ps.acquire("pe", bank=4 + i) for i in range(4)]
    sq = [pc(g, 0, 2048), pc(g, 2048, 2048)]
    for ec in range(NE):
        xc = xres[:, ec * T:(ec + 1) * T]
        buf = sq[ec % 2]
        if ec >= 2:
            em.wait("dve", "ln_sqpe", sqpe_base + ec - 1)
        em.step("dve", lambda E, o=buf, a=xc: E.tensor_tensor(o, a, a, ALU.mult))
        em.mark("dve", "ln_dvm")
        em.wait("pe", "ln_dvm", em.cnt["ln_dvm"])
        for tt in range(4):
            em.step("pe", lambda E, o=g.ps.t[banks_s[tt]][0:1, :], l=ones,
                    r=xc[:, tt * 512:(tt + 1) * 512],
                    s=(ec == 0), p=(ec == NE - 1): E.matmul(o, l, r, start=s, stop=p))
            em.step("pe", lambda E, o=g.ps.t[banks_q[tt]][0:1, :], l=ones,
                    r=buf[:, tt * 512:(tt + 1) * 512],
                    s=(ec == 0), p=(ec == NE - 1): E.matmul(o, l, r, start=s, stop=p))
        em.mark("pe", "ln_sqpe")
    em.mark("pe", "ln_pem")
    em.wait("dve", "ln_pem", em.cnt["ln_pem"])

    def row(i):
        off = 6144 + i * 512
        return g.sb["pool"][0:1, off:off + 512]
    A = g.sb["pool"][:, 0:T]
    B = g.sb["pool"][:, 2048:2048 + T]
    for tt in range(4):
        pss = g.ps.t[banks_s[tt]][0:1, :]
        psq = g.ps.t[banks_q[tt]][0:1, :]
        em.step("dve", lambda E, o=row(0), i=pss: E.tensor_scalar(o, i, 1.0 / 2048.0, None, ALU.mult))
        em.gap("dve", g)
        em.step("dve", lambda E, o=row(1), i=psq: E.tensor_scalar(o, i, 1.0 / 2048.0, None, ALU.mult))
        em.gap("dve", g)
        em.step("dve", lambda E, o=row(2), a=row(0): E.tensor_tensor(o, a, a, ALU.mult))
        em.gap("dve", g)
        em.step("dve", lambda E, o=row(2), a=row(1), b=row(2): E.tensor_tensor(o, a, b, ALU.subtract))
        em.gap("dve", g)
        em.step("dve", lambda E, o=row(2), i=row(2): E.tensor_scalar(o, i, 1e-5, None, ALU.add))
        em.mark("dve", "ln_dvm")
        em.wait("act", "ln_dvm", em.cnt["ln_dvm"])
        em.step("act", lambda E, o=row(1), i=row(2): E.activation(o, i, ACTF.Sqrt))
        em.gap("act", g, inc="ln_acq")
        em.wait("dve", "ln_acq", em.cnt["ln_acq"])
        em.step("dve", lambda E, o=row(3), i=row(1): E.reciprocal(o, i))
        em.gap("dve", g)
        em.gap("dve", g)
        em.step("dve", lambda E, o=row(4), i=row(0), a=row(3):
                E.scalar_tensor_tensor(o, i, -1.0, a, ALU.mult, ALU.mult))
        if "lnr2" in g.tap_t and tag == "ln1":
            em.step("dve", lambda E, o=g.sb["pool"][0:1, 8704 + tt * 1024:8704 + tt * 1024 + 512],
                    i=row(3): E.tensor_copy(out=o, in_=i))
            em.step("dve", lambda E, o=g.sb["pool"][0:1, 8704 + tt * 1024 + 512:8704 + tt * 1024 + 1024],
                    i=row(4): E.tensor_copy(out=o, in_=i))
        em.mark("dve", "ln_dvm")
        g.ps.release(banks_s[tt], "dve", lambda E: E.drain())
        g.ps.release(banks_q[tt], "dve", lambda E: E.drain())
        em.wait("pe", "ln_dvm", em.cnt["ln_dvm"])
        for (ri, dstbuf), bk in (((3, A), banks_s[tt]), ((4, B), banks_q[tt])):
            bank = g.ps.acquire("pe", bank=bk)
            em.step("pe", lambda E, o=g.ps.t[bank][:, :], l=ones1x, r=row(ri):
                    E.matmul(o, l, r, start=True, stop=True), inc="ln_pem")
            em.wait("dve", "ln_pem", em.cnt["ln_pem"])
            g.ps.release(bank, "dve", lambda E, o=dstbuf[:, tt * 512:(tt + 1) * 512],
                         i=g.ps.t[bank][:, :]: E.tensor_copy(out=o, in_=i))
    tmp = g.sb["pool"][:, 4096:4096 + T]
    for ec in range(NE):
        xc = xres[:, ec * T:(ec + 1) * T]
        if spill:
            sv = em.dma("act", g.s_xres[ec], xc, inc="ln_gq")
            em.wait("dve", "ln_gq", sv)
        em.step("dve", lambda E, o=tmp, a=xc, b=A: E.tensor_tensor(o, a, b, ALU.mult))
        em.step("dve", lambda E, o=xc, a=tmp, b=B: E.tensor_tensor(o, a, b, ALU.add),
                inc="ln_dvm")
    em.mark("dve", "dvm")
    g.ln_done = ("dvm", em.cnt["dvm"])



def p_qkv(g):
    em, sb, d = g.em, g.sb, g.d
    u = sb["xres"][:]
    ln_sem, ln_val = g.ln_done
    tables = {}
    lotmp = pc(g, 12416, 1024)
    em.wait("sp", ln_sem, ln_val)
    for ti, nm in enumerate(("qcos", "qsin", "kcos", "ksin")):
        dst = pc(g, 6144 + ti * 1024, 1024)
        em.dma("sp", dst, d[nm + "_hi"][:, :], inc="spq")
        em.dma("sp", lotmp, d[nm + "_lo"][:, :], inc="spq")
        em.wait("dve", "spq", em.cnt["spq"])
        em.wait("dve", "spq", em.cnt["spq"])
        em.step("dve", lambda E, o=dst, a=dst, b=lotmp: E.tensor_tensor(o, a, b, ALU.add),
                inc="dvm")
        tables[nm] = dst
    wt = pc(g, 0, 6144)
    stg = [pc(g, 10240, 384), pc(g, 10624, 384)]
    stg2 = [pc(g, 11008, 256), pc(g, 11264, 256)]
    tmp1 = pc(g, 11520, 64)
    tmp2 = pc(g, 11584, 64)
    qbuf = pc(g, 11648, 256)
    kbuf = pc(g, 11904, 256)
    vbuf = pc(g, 12160, 256)
    wqkvT4 = d["wqkvT"].rearrange("(c p) (g hh w) -> p c g hh w", p=128, g=3, w=128)
    cid = pc(g, ID_OFF, 128)
    for h in range(NH):
        if h >= 1:
            em.wait("sp", "a2_wpe", em.cnt["a2_wpe"])
        for gi in range(3):
            wv = em.dma("sp", wt.rearrange("p (c g w) -> p c g w", g=3, w=128)[:, :, gi, :],
                        wqkvT4[:, :, gi, h, :], inc="spq")
        em.wait("pe", "spq", wv)
        if h == 0 and "wt0" in g.tap_t:
            em.wait("act", "spq", wv)
            em.dma("act", g.tap_t["wt0"], wt, inc="gq")
        if h == 0:
            em.wait("pe", ln_sem, ln_val)
            em.wait("dve", "dvm", em.cnt["dvm"])
        for tb in range(16):
            sA = stg[tb % 2]
            sB = stg2[tb % 2]
            bank = g.ps.acquire("pe")
            pst = g.ps.t[bank]
            for ec in range(NE):
                em.step("pe", lambda E, o=pst[:, 0:384],
                        l=u[:, ec * T + tb * 128: ec * T + tb * 128 + 128],
                        r=wt[:, ec * 384:(ec + 1) * 384], s=(ec == 0), p=(ec == NE - 1):
                        E.matmul(o, l, r, start=s, stop=p), inc=("peq" if ec == NE - 1 else None))
            em.wait("dve", "peq", em.cnt["peq"])
            g.ps.release(bank, "dve", lambda E, o=sA, i=pst[:, 0:384]: E.tensor_copy(out=o, in_=i))
            qe, qo = sA[:, 0:64], sA[:, 64:128]
            ke, ko = sA[:, 128:192], sA[:, 192:256]
            tq_c = tables["qcos"][:, tb * 64:(tb + 1) * 64]
            tq_s = tables["qsin"][:, tb * 64:(tb + 1) * 64]
            tk_c = tables["kcos"][:, tb * 64:(tb + 1) * 64]
            tk_s = tables["ksin"][:, tb * 64:(tb + 1) * 64]
            for (xe, xo, tc, tsn, dst_off) in ((qe, qo, tq_c, tq_s, 0), (ke, ko, tk_c, tk_s, 128)):
                de = sB[:, dst_off:dst_off + 64]
                do = sB[:, dst_off + 64:dst_off + 128]
                em.step("dve", lambda E, o=tmp1, a=xe, b=tc: E.tensor_tensor(o, a, b, ALU.mult))
                em.step("dve", lambda E, o=tmp2, a=xo, b=tsn: E.tensor_tensor(o, a, b, ALU.mult))
                em.step("dve", lambda E, o=de, a=tmp1, b=tmp2: E.tensor_tensor(o, a, b, ALU.subtract))
                em.step("dve", lambda E, o=tmp1, a=xe, b=tsn: E.tensor_tensor(o, a, b, ALU.mult))
                em.step("dve", lambda E, o=tmp2, a=xo, b=tc: E.tensor_tensor(o, a, b, ALU.mult))
                em.step("dve", lambda E, o=do, a=tmp1, b=tmp2: E.tensor_tensor(o, a, b, ALU.add))
            if tb >= 4:
                em.wait("dve", "gq", (em.cnt.get("gq", 0)))
            em.step("dve", lambda E, o=vbuf[:, (tb % 2) * 128:(tb % 2) * 128 + 128],
                    i=sA[:, 256:384]: E.tensor_copy(out=o, in_=i))
            em.mark("dve", "dvm")
            em.wait("pe", "dvm", em.cnt["dvm"])
            for (src_off, dst) in ((0, qbuf), (128, kbuf)):
                bank2 = g.ps.acquire("pe")
                tv = em.step("pe", lambda E, o=g.ps.t[bank2][:, 0:128], i=sB[:, src_off:src_off + 128],
                             ident=cid: E.transpose(o, i, ident), inc="peq")
                em.wait("dve", "peq", tv)
                g.ps.release(bank2, "dve", lambda E, o=dst[:, (tb % 2) * 128:(tb % 2) * 128 + 128],
                             i=g.ps.t[bank2][:, 0:128]: E.tensor_copy(out=o, in_=i))
            if tb % 2 == 1:
                em.mark("dve", "dvm")
                pv = em.cnt["dvm"]
                em.wait("act", "dvm", pv)
                em.dma("act", g.s_q[h, tb - 1:tb + 1].rearrange("b p w -> p b w"),
                       qbuf.rearrange("p (b w) -> p b w", w=128), inc="gq")
                em.dma("act", g.s_k[h, tb - 1:tb + 1].rearrange("b p w -> p b w"),
                       kbuf.rearrange("p (b w) -> p b w", w=128), inc="gq")
                em.dma("act", g.s_vt[h, tb - 1:tb + 1].rearrange("b p w -> p b w"),
                       vbuf.rearrange("p (b w) -> p b w", w=128), inc="gq")
        em.mark("pe", "a2_wpe")
    em.wait("act", "gq", em.cnt["gq"])
    em.mark("act", "gq")
    g.a2_done = ("gq", em.cnt["gq"])



def p_attn(g):
    em, sb, d = g.em, g.sb, g.d
    xres = sb["xres"][:]
    cid = pc(g, ID_OFF, 128)
    m01 = pc(g, 14336, 128)
    mneg = pc(g, 14464, 128)
    em.dma("sp", m01, d["m01"][:, :], inc="spq")
    em.dma("sp", mneg, d["mneg"][:, :], inc="spq")
    negmax = g.sb["pool"][:, 14592:14593]
    rowsum = g.sb["pool"][:, 14593:14594]
    recip = g.sb["pool"][:, 14594:14595]
    tmpr = g.sb["pool"][:, 14595:14596]
    tmpr2 = g.sb["pool"][:, 14596:14597]
    tmpr3 = g.sb["pool"][:, 14597:14598]
    em.wait("sp", "a2_wpe", NH)
    for ec in range(NE):
        em.dma("sp", xres[:, ec * T:(ec + 1) * T], g.s_xres[ec], inc="spq")
    em.wait("dve", "spq", em.cnt["spq"])
    em.wait("dve", "spq", em.cnt["spq"])
    qh = pc(g, 0, 2048)
    kh = pc(g, 2048, 2048)
    vt = pc(g, 4096, 2048)
    pstage = pc(g, 6144, 2048)
    ptr = pc(g, 8192, 2048)
    ybuf = pc(g, 10240, 2048)
    wp = pc(g, 12288, 2048)
    a2v = g.a2_done
    em.wait("sp", a2v[0], a2v[1])
    import os
    nh_lim = int(os.environ.get("DBG_NH", NH))
    for h in range(nh_lim):
        if h >= 1:
            em.wait("sp", "a3_prj", em.cnt["a3_prj"])
        em.dma("sp", qh.rearrange("p (b w) -> p b w", w=128), g.s_q[h].rearrange("b p w -> p b w"), inc="spq")
        em.dma("sp", kh.rearrange("p (b w) -> p b w", w=128), g.s_k[h].rearrange("b p w -> p b w"), inc="spq")
        em.dma("sp", vt.rearrange("p (b w) -> p b w", w=128), g.s_vt[h].rearrange("b p w -> p b w"), inc="spq")
        em.dma("sp", wp, d["wprojT"][h * 128:(h + 1) * 128, :], inc="spq")
        em.wait("pe", "spq", em.cnt["spq"])
        for qb in range(16):
            ncols = (qb + 1) * 128
            nch = (ncols + 511) // 512
            if h > 0 or qb > 0:
                em.wait("act", "peq", em.cnt["peq"])
                em.wait("dve", "peq", em.cnt["peq"])
            sc_banks = []
            for ci in range(nch):
                nk = min(512, ncols - ci * 512)
                bank = g.ps.acquire("pe", bank=ci)
                sc_banks.append((bank, nk))
                em.step("pe", lambda E, o=g.ps.t[bank][:, 0:nk], l=qh[:, qb * 128:(qb + 1) * 128],
                        r=kh[:, ci * 512:ci * 512 + nk]: E.matmul(o, l, r, start=True, stop=True),
                        inc="peq")
            em.wait("dve", "peq", em.cnt["peq"])
            dci = (ncols - 128) // 512
            doff = (qb * 128) % 512
            dps = g.ps.t[sc_banks[dci][0]][:, doff:doff + 128]
            em.step("dve", lambda E, o=dps, a=dps, b=m01: E.tensor_tensor(o, a, b, ALU.mult))
            em.step("dve", lambda E, o=dps, a=dps, b=mneg: E.tensor_tensor(o, a, b, ALU.add))
            if h > 0 or qb > 0:
                em.wait("dve", "peq", em.cnt["peq"])
            for ci, (bank, nk) in enumerate(sc_banks):
                g.ps.release(bank, "dve", lambda E, o=pstage[:, ci * 512:ci * 512 + nk],
                             i=g.ps.t[bank][:, 0:nk]: E.tensor_copy(out=o, in_=i))
            em.step("dve", lambda E, o=ptr[:, 0:ncols], i=pstage[:, 0:ncols], a=negmax:
                    E.tensor_scalar(o, i, -1.0, 3.4028235e38, ALU.mult, ALU.min, accum_out=a))
            em.mark("dve", "dvm")
            em.wait("act", "dvm", em.cnt["dvm"])
            for ci, (bank, nk) in enumerate(sc_banks):
                em.step("act", lambda E, o=ptr[:, ci * 512:ci * 512 + nk],
                        i=pstage[:, ci * 512:ci * 512 + nk],
                        b=negmax: E.activation(o, i, ACTF.Exp, bias=b))
            em.mark("act", "acq")
            em.wait("dve", "acq", em.cnt["acq"])
            em.step("dve", lambda E, o=pstage[:, 0:ncols], i=ptr[:, 0:ncols], a=rowsum:
                    E.tensor_scalar(o, i, 1.0, 0.0, ALU.mult, ALU.add, accum_out=a))
            em.step("dve", lambda E: E.drain())
            em.step("dve", lambda E, o=recip, i=rowsum: E.reciprocal(o, i))
            em.gap("dve", g)
            em.step("dve", lambda E, o=ptr[:, 0:ncols], i=ptr[:, 0:ncols], s=recip:
                    E.tensor_scalar(o, i, s, None, ALU.mult))
            em.mark("dve", "dvm")
            if h == 0 and qb == 0 and "dbg3" in g.tap_t:
                em.step("dve", lambda E, o=g.sb["pool"][:, 14600:14728], i=pstage[:, 0:128]:
                        E.tensor_copy(out=o, in_=i))
                em.step("dve", lambda E, o=g.sb["pool"][:, 14728:14729], i=negmax:
                        E.tensor_copy(out=o, in_=i))
                em.step("dve", lambda E, o=g.sb["pool"][:, 14729:14730], i=rowsum:
                        E.tensor_copy(out=o, in_=i))
                em.step("dve", lambda E, o=g.sb["pool"][:, 14730:14858], i=ptr[:, 0:128]:
                        E.tensor_copy(out=o, in_=i), inc="dbg3w")
            em.wait("pe", "dvm", em.cnt["dvm"])
            for kb in range(qb + 1):
                bank = g.ps.acquire("pe", bank=4 + kb % 2)
                tv = em.step("pe", lambda E, o=g.ps.t[bank][:, 0:128], i=ptr[:, kb * 128:(kb + 1) * 128],
                             ident=cid: E.transpose(o, i, ident), inc="peq")
                em.wait("dve", "peq", tv)
                g.ps.release(bank, "dve", lambda E, o=pstage[:, kb * 128:(kb + 1) * 128],
                             i=g.ps.t[bank][:, 0:128]: E.tensor_copy(out=o, in_=i))
            em.mark("dve", "dvm")
            em.wait("pe", "dvm", em.cnt["dvm"])
            ybank = g.ps.acquire("pe", bank=6)
            for kb in range(qb + 1):
                em.step("pe", lambda E, o=g.ps.t[ybank][:, 0:128], l=vt[:, kb * 128:(kb + 1) * 128],
                        r=pstage[:, kb * 128:(kb + 1) * 128], s=(kb == 0), pp=(kb == qb):
                        E.matmul(o, l, r, start=s, stop=pp), inc=("peq" if kb == qb else None))
            em.wait("dve", "peq", em.cnt["peq"])
            g.ps.release(ybank, "dve", lambda E, o=ybuf[:, qb * 128:(qb + 1) * 128],
                         i=g.ps.t[ybank][:, 0:128]: E.tensor_copy(out=o, in_=i))
            if h == 0 and qb == 0 and "dbg1" in g.tap_t:
                em.step("dve", lambda E, o=g.sb["pool"][:, 14860:14988], i=ptr[:, 0:128]:
                        E.tensor_copy(out=o, in_=i))
                em.step("dve", lambda E, o=g.sb["pool"][:, 14988:15116], i=vt[:, 0:128]:
                        E.tensor_copy(out=o, in_=i))
                em.step("dve", lambda E, o=g.sb["pool"][:, 15116:15244], i=ybuf[:, 0:128]:
                        E.tensor_copy(out=o, in_=i), inc="dbg1w")
        em.mark("dve", "dvm")
        em.wait("pe", "dvm", em.cnt["dvm"])
        for ec in range(NE):
            for tt in range(4):
                bank = g.ps.acquire("pe")
                pv = em.step("pe", lambda E, o=g.ps.t[bank][:, :], l=wp[:, ec * 128:(ec + 1) * 128],
                             r=ybuf[:, tt * 512:(tt + 1) * 512]: E.matmul(o, l, r, start=True, stop=True),
                             inc="peq")
                em.wait("dve", "peq", pv)
                dst = xres[:, ec * T + tt * 512: ec * T + tt * 512 + 512]
                g.ps.release(bank, "dve", lambda E, o=dst, a=g.ps.t[bank][:, :], b=dst:
                             E.tensor_tensor(o, a, b, ALU.add))
        em.mark("pe", "a3_prj")
    em.mark("dve", "dvm")
    g.a3_done = ("dvm", em.cnt["dvm"])
    g.ln_done = None


def p_ffn_build(g):
    em, sb, d = g.em, g.sb, g.d
    lutrep = g.sb["pool"][:, 14592:14608]
    ones1x = g.sb["pool"][0:1, ID_OFF + 130:ID_OFF + 258]
    lutrow = g.sb["pool"][0:1, 14608:14624]
    em.dma("sp", lutrow, d["lut"][:, :], inc="spq")
    em.wait("pe", "spq", em.cnt["spq"])
    bank = g.ps.acquire("pe")
    em.step("pe", lambda E, o=g.ps.t[bank][:, 0:16], l=ones1x, r=lutrow:
            E.matmul(o, l, r, start=True, stop=True), inc="peq")
    em.wait("dve", "peq", em.cnt["peq"])
    g.ps.release(bank, "dve", lambda E, o=lutrep, i=g.ps.t[bank][:, 0:16]: E.tensor_copy(out=o, in_=i))
    em.mark("dve", "dvm")
    em.wait("sp", g.ln_done[0], g.ln_done[1])
    em.wait("dve", g.ln_done[0], g.ln_done[1])
    idx_i = sb["pool"][:, 0:1024].bitcast(I32)
    idx_b = sb["pool"][:, 1024:1536].bitcast(BF16)
    masks = sb["pool"][:, 2048:2048 + 16 * 512].bitcast(BF16)
    acc = sb["pool"][:, 10240:11264]
    jobs = [(d["gidxT"], g.s_wg, NE, INTER), (d["uidxT"], g.s_wu, NE, INTER),
            (d["didxT"], g.s_wd, NI, E)]
    for (src, dst, nrc, rlen) in jobs:
        nft = (rlen + 1023) // 1024
        for rc in range(nrc):
            for ft in range(nft):
                fw = min(1024, rlen - ft * 1024)
                if em.cnt.get("gq", 0) > 0:
                    em.wait("sp", "gq", em.cnt["gq"])
                lv = em.dma("sp", idx_i[:, 0:fw], src[rc * 128:(rc + 1) * 128, ft * 1024:ft * 1024 + fw],
                            inc="spq")
                em.wait("dve", "spq", lv)
                em.wait("dve", "dvm", em.cnt["dvm"])
                em.step("dve", lambda E, o=idx_b[:, 0:fw], i=idx_i[:, 0:fw]: E.tensor_copy(out=o, in_=i))
                for j in range(16):
                    em.step("dve", lambda E, o=masks[:, j * 1024:j * 1024 + fw], i=idx_b[:, 0:fw], jj=float(j):
                            E.tensor_scalar(o, i, jj, None, ALU.is_equal))
                em.step("dve", lambda E, o=acc[:, 0:fw], i=masks[:, 0:fw], s=lutrep[:, 0:1]:
                        E.tensor_scalar(o, i, s, None, ALU.mult))
                for j in range(1, 16):
                    em.step("dve", lambda E, o=acc[:, 0:fw], i=masks[:, j * 1024:j * 1024 + fw],
                            s=lutrep[:, j:j + 1], a=acc[:, 0:fw]:
                            E.scalar_tensor_tensor(o, i, s, a, ALU.mult, ALU.add))
                em.mark("dve", "dvm")
                em.wait("act", "dvm", em.cnt["dvm"])
                em.dma("act", dst[rc, :, ft * 1024:ft * 1024 + fw], acc[:, 0:fw], inc="gq")
                em.wait("dve", "gq", em.cnt["gq"])
    em.wait("act", "gq", em.cnt["gq"])
    em.mark("act", "gq")
    g.fb_done = ("gq", em.cnt["gq"])


def p_ffn(g, tag, out_mode):
    em, sb, d = g.em, g.sb, g.d
    xn = sb["xres"][:]
    cid = pc(g, ID_OFF, 128)
    wbuf = [pc(g, 0, 2048), pc(g, 2048, 2048)]
    hbuf = pc(g, 4096, 2048)
    stmp = pc(g, 6144, 2048)
    rbuf = pc(g, 8192, 1024)
    obuf = pc(g, 9216, 2048)
    wsrc = {0: g.s_wg, 1: g.s_wu}
    em.wait("sp", g.fb_done[0], g.fb_done[1])
    em.wait("sp", g.ln_done[0], g.ln_done[1])
    gu_base = em.cnt.get("fn_gu_pe", 0)
    rese_base = em.cnt.get("fn_rese", 0)
    for fc in range(NI):
        for part in range(2):
            bi = (fc * 2 + part) % 2
            if fc * 2 + part >= 2:
                em.wait("sp", "fn_gu_pe", em.cnt["fn_gu_pe"])
            wv = em.dma("sp", wbuf[bi].rearrange("p (c w) -> p c w", w=128),
                        wsrc[part][:, :, fc * 128:(fc + 1) * 128].rearrange("c p w -> p c w"),
                        inc="spq")
            em.wait("pe", "spq", wv)
            if fc == 0 and part == 0:
                em.wait("pe", g.ln_done[0], g.ln_done[1])
                em.wait("pe", g.fb_done[0], g.fb_done[1])
            banks = [g.ps.acquire("pe", bank=part * 4 + tt) for tt in range(4)]
            for ec in range(NE):
                for tt in range(4):
                    em.step("pe", lambda E, o=g.ps.t[banks[tt]][:, :], l=wbuf[bi][:, ec * 128:(ec + 1) * 128],
                            r=xn[:, ec * T + tt * 512:ec * T + tt * 512 + 512],
                            s=(ec == 0), pp=(ec == NE - 1):
                            E.matmul(o, l, r, start=s, stop=pp))
            em.mark("pe", "peq")
            em.mark("pe", "fn_gu_pe")
            if part == 1:
                em.wait("act", "peq", em.cnt["peq"])
                em.wait("act", "dvm", em.cnt["dvm"])
                for tt in range(4):
                    em.step("act", lambda E, o=stmp[:, tt * 512:(tt + 1) * 512], i=g.ps.t[tt][:, :]:
                            E.activation(o, i, ACTF.Silu))
                em.mark("act", "acq")
                em.wait("dve", "acq", em.cnt["acq"])
                if fc >= 1:
                    em.wait("dve", "gq", em.cnt["gq"])
                for tt in range(4):
                    g.ps.release(tt, "dve", lambda E: E.drain())
                    g.ps.release(4 + tt, "dve",
                                 lambda E, o=hbuf[:, tt * 512:(tt + 1) * 512],
                                 a=stmp[:, tt * 512:(tt + 1) * 512], b=g.ps.t[4 + tt][:, :]:
                                 E.tensor_tensor(o, a, b, ALU.mult))
                em.mark("dve", "dvm")
                em.wait("act", "dvm", em.cnt["dvm"])
                em.dma("act", g.s_h[fc], hbuf, inc="gq")
    em.wait("sp", "gq", em.cnt["gq"])
    res_iter = [0]
    for tt in range(4):
        for eg in range(2):
            banks = [g.ps.acquire("pe", bank=i) for i in range(8)]
            for hc in range(NI):
                bi = hc % 2
                if tt + eg + hc > 0:
                    em.wait("sp", "fn_dn_pe", em.cnt["fn_dn_pe"])
                wv = em.dma("sp", wbuf[bi].rearrange("p (c w) -> p c w", w=128)[:, 0:8, :],
                            g.s_wd[hc, :, eg * 1024:(eg + 1) * 1024].rearrange("p (c w) -> p c w", w=128),
                            inc="spq")
                hv = em.dma("sp", hbuf[:, bi * 512:(bi + 1) * 512] if False else (stmp[:, bi * 512:(bi + 1) * 512]),
                            g.s_h[hc, :, tt * 512:(tt + 1) * 512], inc="spq")
                em.wait("pe", "spq", wv)
                em.wait("pe", "spq", hv)
                if tt == 0 and eg == 0 and hc == 0:
                    em.wait("pe", "gq", em.cnt["gq"])
                for ei in range(8):
                    em.step("pe", lambda E, o=g.ps.t[banks[ei]][:, :],
                            l=wbuf[bi][:, ei * 128:(ei + 1) * 128],
                            r=stmp[:, bi * 512:(bi + 1) * 512],
                            s=(hc == 0), pp=(hc == NI - 1):
                            E.matmul(o, l, r, start=s, stop=pp))
                em.mark("pe", "fn_dn_pe")
            em.mark("pe", "peq")
            em.wait("dve", "peq", em.cnt["peq"])
            for ei in range(8):
                ec = eg * 8 + ei
                slot = rbuf[:, (ei % 2) * 512:(ei % 2) * 512 + 512]
                if res_iter[0] >= 2:
                    em.wait("sp", "fn_rese", em.cnt["fn_rese"])
                res_iter[0] += 1
                if out_mode == "xres":
                    rv = em.dma("sp", slot, g.s_xres[ec, :, tt * 512:(tt + 1) * 512],
                                inc="spq")
                    em.wait("dve", "spq", rv)
                    dst = xn[:, ec * T + tt * 512:ec * T + tt * 512 + 512]
                    g.ps.release(banks[ei], "dve", lambda E, o=dst, a=g.ps.t[banks[ei]][:, :],
                                 b=slot: E.tensor_tensor(o, a, b, ALU.add))
                    em.mark("dve", "fn_rese")
                else:
                    rv = em.dma("sp", slot, g.s_xres[ec, :, tt * 512:(tt + 1) * 512],
                                inc="spq")
                    em.wait("dve", "spq", rv)
                    em.step("dve", lambda E, o=obuf[:, 0:512], a=g.ps.t[banks[ei]][:, :],
                            b=slot: E.tensor_tensor(o, a, b, ALU.add))
                    em.mark("dve", "fn_rese")
                    g.ps.release(banks[ei], "dve", lambda E: E.drain())
                    em.mark("dve", "dvm")
                    em.wait("pe", "dvm", em.cnt["dvm"])
                    for sb4 in range(4):
                        bank2 = g.ps.acquire("pe", bank=banks[ei])
                        tv = em.step("pe", lambda E, o=g.ps.t[bank2][:, 0:128],
                                     i=obuf[:, sb4 * 128:sb4 * 128 + 128], ident=cid:
                                     E.transpose(o, i, ident), inc="peq")
                        em.wait("dve", "peq", tv)
                        g.ps.release(bank2, "dve", lambda E, o=obuf[:, 512 + sb4 * 128: 512 + sb4 * 128 + 128],
                                     i=g.ps.t[bank2][:, 0:128]: E.tensor_copy(out=o, in_=i))
                    em.mark("dve", "dvm")
                    em.wait("act", "dvm", em.cnt["dvm"])
                    em.dma("act", g.out[tt * 512:(tt + 1) * 512, ec * 128:(ec + 1) * 128]
                           .rearrange("(b p) w -> p b w", p=128),
                           obuf[:, 512:1024].rearrange("p (b w) -> p b w", w=128), inc="outd")
                    em.wait("dve", "outd", em.cnt["outd"])
    em.mark("dve", "dvm")
    g.ffn_done = ("dvm", em.cnt["dvm"])



def p_ssm_proj(g):
    em, sb, d = g.em, g.sb, g.d
    u = sb["xres"][:]
    wbuf = [pc(g, 0, 2048), pc(g, 2048, 2048)]
    padbuf = pc(g, 4096, 2176)
    kcbuf = pc(g, 6272, 2048)
    vbuf = pc(g, 8320, 2048)
    kvbuf = pc(g, 10368, 2048)
    em.wait("sp", g.ln_done[0], g.ln_done[1])
    em.wait("dve", g.ln_done[0], g.ln_done[1])
    em.step("dve", lambda E, a=padbuf[:, 0:2]: E.memset(a, 0.0))
    wmap = [("wqT", 6, True, None), ("wkT", 7, True, None), ("wvT", 8, False, None)]
    it = [0]
    for eo in range(NE):
        for (wnm, bcol, do_conv, _) in wmap:
            bi = it[0] % 2
            if it[0] >= 2:
                em.wait("sp", "sp_pe", em.cnt["sp_pe"])
            wv = em.dma("sp", wbuf[bi].rearrange("p (c w) -> p c w", w=128),
                        d[wnm][:, eo * 128:(eo + 1) * 128].rearrange("(c p) w -> p c w", p=128),
                        inc="spq")
            em.wait("pe", "spq", wv)
            banks = [g.ps.acquire("pe", bank=tt) for tt in range(4)]
            for ec in range(NE):
                for tt in range(4):
                    em.step("pe", lambda E, o=g.ps.t[banks[tt]][:, :],
                            l=wbuf[bi][:, ec * 128:(ec + 1) * 128],
                            r=u[:, ec * T + tt * 512:ec * T + tt * 512 + 512],
                            s=(ec == 0), pp=(ec == NE - 1): E.matmul(o, l, r, start=s, stop=pp))
            em.mark("pe", "peq")
            em.mark("pe", "sp_pe")
            it[0] += 1
            em.wait("act", "peq", em.cnt["peq"])
            tgt = padbuf[:, 2:2050] if do_conv else vbuf
            for tt in range(4):
                em.step("act", lambda E, o=tgt[:, tt * 512:(tt + 1) * 512], i=g.ps.t[banks[tt]][:, :],
                        bb=small_col(g, eo, bcol): E.activation(o, i, ACTF.Identity, bias=bb))
            em.mark("act", "acq")
            em.wait("dve", "acq", em.cnt["acq"])
            for tt in range(4):
                g.ps.release(banks[tt], "dve", lambda E: E.drain())
            if do_conv:
                is_q = (wnm == "wqT")
                dst = kvbuf if is_q else kcbuf
                w0 = small_col(g, eo, 0 if is_q else 3)
                w1 = small_col(g, eo, 1 if is_q else 4)
                w2 = small_col(g, eo, 2 if is_q else 5)
                cb = small_col(g, eo, 11 if is_q else 12)
                em.step("dve", lambda E, o=dst, i=padbuf[:, 0:2048], s1=w0, s2=cb:
                        E.tensor_scalar(o, i, s1, s2, ALU.mult, ALU.add))
                em.step("dve", lambda E, o=dst, i=padbuf[:, 1:2049], s=w1, a=dst:
                        E.scalar_tensor_tensor(o, i, s, a, ALU.mult, ALU.add))
                em.step("dve", lambda E, o=dst, i=padbuf[:, 2:2050], s=w2, a=dst:
                        E.scalar_tensor_tensor(o, i, s, a, ALU.mult, ALU.add), inc="dvm")
                if is_q:
                    em.wait("act", "dvm", em.cnt["dvm"])
                    em.dma("act", g.s_qc[eo], dst, inc="gq")
                    em.wait("dve", "gq", em.cnt["gq"])
        em.step("dve", lambda E, o=kvbuf, a=kcbuf, b=vbuf: E.tensor_tensor(o, a, b, ALU.mult),
                inc="dvm")
        em.wait("act", "dvm", em.cnt["dvm"])
        em.dma("act", g.s_kv[eo], kvbuf, inc="gq")
        em.wait("dve", "gq", em.cnt["gq"])
    em.mark("act", "gq")
    g.sp_done = ("gq", em.cnt["gq"])


FC = {
    "f1re": (0, 4096), "f1im": (4096, 4096),
    "f2re": (8192, 128), "f2imn": (8320, 128), "f2im": (8448, 128),
    "are": (8576, 128), "aim": (8704, 128), "aimn": (8832, 128),
    "twre": (8960, 128), "twim": (9088, 128),
    "bcos": (9216, 64), "bnsin": (9280, 64),
}
QC_OFF = 9472


def p_ssm_fft(g):
    em, sb, d = g.em, g.sb, g.d
    R = sb["xres"][:]
    cid = pc(g, ID_OFF, 128)
    em.wait("sp", g.sp_done[0], g.sp_done[1])
    em.wait("dve", g.sp_done[0], g.sp_done[1])
    ctmp = pc(g, QC_OFF, 4096)
    for nm, (off, w) in FC.items():
        npart = 64 if nm in ("f1re", "f1im") else 128
        dst = pc(g, off, w)[0:npart, :]
        em.dma("sp", dst, d[nm + "_hi"][:, :], inc="spq")
        lv = em.dma("sp", ctmp[0:npart, 0:w], d[nm + "_lo"][:, :], inc="spq")
        em.wait("dve", "spq", lv)
        em.step("dve", lambda E, o=dst, a=dst, b=ctmp[0:npart, 0:w]: E.tensor_tensor(o, a, b, ALU.add),
                inc="dvm")
    em.mark("dve", "dvm")
    fcv = ("dvm", em.cnt["dvm"])
    kvb = R[:, 0:2048]
    st0 = R[0:64, 2048:6144]
    temp = R[:, 2048:6144]
    ybuf = R[:, 4096:6144]
    s1re, s1im = R[:, 6144:10240], R[:, 10240:14336]
    s2re, s2im = R[:, 14336:18432], R[:, 18432:22528]
    Xre, Xim = R[:, 22528:26624], R[:, 26624:30720]
    qcb = pc(g, QC_OFF, 2048)
    f1re, f1im = pc(g, *FC["f1re"]), pc(g, *FC["f1im"])
    f2re, f2im, f2imn = pc(g, *FC["f2re"]), pc(g, *FC["f2im"]), pc(g, *FC["f2imn"])
    are, aim, aimn = pc(g, *FC["are"]), pc(g, *FC["aim"]), pc(g, *FC["aimn"])
    twre, twim = pc(g, *FC["twre"]), pc(g, *FC["twim"])
    bcos, bnsin = pc(g, *FC["bcos"]), pc(g, *FC["bnsin"])

    def stage12(src_re_dst=None, kernel_mode=False):
        for b in range(32):
            rhs = st0[:, b * 128:(b + 1) * 128]
            for (tbl, dst) in ((f1re, s1re), (f1im, s1im)):
                bank = g.ps.acquire("pe")
                mv = em.step("pe", lambda E, o=g.ps.t[bank][:, 0:128],
                             l=tbl[0:64, b * 128:(b + 1) * 128], r=rhs:
                             E.matmul(o, l, r, start=True, stop=True), inc="peq")
                em.wait("dve", "peq", mv)
                dstap = dst.rearrange("p (c b2) -> p c b2", b2=32)[:, :, b]
                g.ps.release(bank, "dve", lambda E, o=dstap, i=g.ps.t[bank][:, 0:128]:
                             E.tensor_copy(out=o, in_=i))
        em.mark("dve", "dvm")
        em.wait("pe", "dvm", em.cnt["dvm"])
        for (src, dst) in ((s1re, s2re), (s1im, s2im)):
            for g4 in range(32):
                bank = g.ps.acquire("pe")
                tv = em.step("pe", lambda E, o=g.ps.t[bank][:, 0:128],
                             i=src[:, g4 * 128:(g4 + 1) * 128], ident=cid:
                             E.transpose(o, i, ident), inc="peq")
                em.wait("dve", "peq", tv)
                g.ps.release(bank, "dve", lambda E, o=dst[:, g4 * 128:(g4 + 1) * 128],
                             i=g.ps.t[bank][:, 0:128]: E.tensor_copy(out=o, in_=i))
        em.mark("dve", "dvm")
        em.wait("pe", "dvm", em.cnt["dvm"])
        for grp in range(8):
            sl = slice(grp * 512, (grp + 1) * 512)
            for (dst, l1, l2) in ((Xre, f2re, f2imn), (Xim, f2im, f2re)):
                bank = g.ps.acquire("pe")
                r1 = s2re[:, sl] if dst is Xre else s2re[:, sl]
                em.step("pe", lambda E, o=g.ps.t[bank][:, :], l=l1, r=s2re[:, sl]:
                        E.matmul(o, l, r, start=True, stop=False))
                mv = em.step("pe", lambda E, o=g.ps.t[bank][:, :], l=l2, r=s2im[:, sl]:
                             E.matmul(o, l, r, start=False, stop=True), inc="peq")
                em.wait("dve", "peq", mv)
                g.ps.release(bank, "dve", lambda E, o=dst[:, sl], i=g.ps.t[bank][:, :]:
                             E.tensor_copy(out=o, in_=i))
        em.mark("dve", "dvm")

    for cb in range(NE):
        if cb > 0:
            em.wait("sp", "dvm", em.cnt["dvm"])
        lv = em.dma("sp", st0, d["ker_s1"][:, cb * 4096:(cb + 1) * 4096], inc="spq")
        em.wait("pe", "spq", lv)
        em.wait("pe", fcv[0], fcv[1])
        em.wait("dve", fcv[0], fcv[1])
        stage12(kernel_mode=True)
        em.wait("act", "dvm", em.cnt["dvm"])
        em.dma("act", g.s_ks[cb, 0], Xre, inc="gq")
        em.dma("act", g.s_ks[cb, 1], Xim, inc="gq")
        em.wait("dve", "gq", em.cnt["gq"])
        em.barrier()
    dsk = 10
    em.wait("sp", fcv[0], fcv[1])
    for cb in range(NE):
        lv = em.dma("sp", kvb, g.s_kv[cb], inc="spq")
        qv = em.dma("sp", qcb, g.s_qc[cb], inc="spq")
        em.wait("pe", "spq", lv)
        kv3 = kvb.rearrange("p (a b2) -> p b2 a", b2=32)
        for b in range(32):
            bank = g.ps.acquire("pe")
            tv = em.step("pe", lambda E, o=g.ps.t[bank][0:64, 0:128], i=kv3[:, b], ident=cid:
                         E.transpose(o, i, ident), inc="peq")
            em.wait("dve", "peq", tv)
            g.ps.release(bank, "dve", lambda E, o=st0[:, b * 128:(b + 1) * 128],
                         i=g.ps.t[bank][0:64, 0:128]: E.tensor_copy(out=o, in_=i))
        em.mark("dve", "dvm")
        em.wait("pe", "dvm", em.cnt["dvm"])
        stage12()
        em.wait("sp", "dvm", em.cnt["dvm"])
        em.wait("sp", "dvm", em.cnt["dvm"])
        kr = em.dma("sp", s1re, g.s_ks[cb, 0], inc="spq")
        ki = em.dma("sp", s1im, g.s_ks[cb, 1], inc="spq")
        em.wait("dve", "spq", ki)
        em.wait("dve", "dvm", em.cnt["dvm"])
        em.step("dve", lambda E, o=s2re, a=Xre, b=s1re: E.tensor_tensor(o, a, b, ALU.mult))
        em.step("dve", lambda E, o=temp[:, 0:4096], a=Xim, b=s1im: E.tensor_tensor(o, a, b, ALU.mult))
        em.step("dve", lambda E, o=s2re, a=s2re, b=temp[:, 0:4096]: E.tensor_tensor(o, a, b, ALU.subtract))
        em.step("dve", lambda E, o=s2im, a=Xre, b=s1im: E.tensor_tensor(o, a, b, ALU.mult))
        em.step("dve", lambda E, o=temp[:, 0:4096], a=Xim, b=s1re: E.tensor_tensor(o, a, b, ALU.mult))
        em.step("dve", lambda E, o=s2im, a=s2im, b=temp[:, 0:4096]: E.tensor_tensor(o, a, b, ALU.add))
        em.mark("dve", "dvm")
        em.wait("pe", "dvm", em.cnt["dvm"])
        for grp in range(8):
            sl = slice(grp * 512, (grp + 1) * 512)
            for (dst, l1, l2) in ((Xre, are, aimn), (Xim, aim, are)):
                bank = g.ps.acquire("pe")
                em.step("pe", lambda E, o=g.ps.t[bank][:, :], l=l1, r=s2re[:, sl]:
                        E.matmul(o, l, r, start=True, stop=False))
                mv = em.step("pe", lambda E, o=g.ps.t[bank][:, :], l=l2, r=s2im[:, sl]:
                             E.matmul(o, l, r, start=False, stop=True), inc="peq")
                em.wait("dve", "peq", mv)
                g.ps.release(bank, "dve", lambda E, o=dst[:, sl], i=g.ps.t[bank][:, :]:
                             E.tensor_copy(out=o, in_=i))
        tb_re = twre.unsqueeze(1).broadcast_to((128, 32, 128))
        tb_im = twim.unsqueeze(1).broadcast_to((128, 32, 128))
        X3 = lambda t: t.rearrange("p (g r) -> p g r", r=128)
        em.step("dve", lambda E, o=X3(s1re), a=X3(Xre), b=tb_re: E.tensor_tensor(o, a, b, ALU.mult))
        em.step("dve", lambda E, o=X3(temp[:, 0:4096]), a=X3(Xim), b=tb_im: E.tensor_tensor(o, a, b, ALU.mult))
        em.step("dve", lambda E, o=s1re, a=s1re, b=temp[:, 0:4096]: E.tensor_tensor(o, a, b, ALU.subtract))
        em.step("dve", lambda E, o=X3(s1im), a=X3(Xre), b=tb_im: E.tensor_tensor(o, a, b, ALU.mult))
        em.step("dve", lambda E, o=X3(temp[:, 0:4096]), a=X3(Xim), b=tb_re: E.tensor_tensor(o, a, b, ALU.mult))
        em.step("dve", lambda E, o=s1im, a=s1im, b=temp[:, 0:4096]: E.tensor_tensor(o, a, b, ALU.add))
        em.mark("dve", "dvm")
        em.wait("pe", "dvm", em.cnt["dvm"])
        for (src, dst) in ((s1re, s2re), (s1im, s2im)):
            for g4 in range(32):
                bank = g.ps.acquire("pe")
                tv = em.step("pe", lambda E, o=g.ps.t[bank][:, 0:128],
                             i=src[:, g4 * 128:(g4 + 1) * 128], ident=cid:
                             E.transpose(o, i, ident), inc="peq")
                em.wait("dve", "peq", tv)
                g.ps.release(bank, "dve", lambda E, o=dst[:, g4 * 128:(g4 + 1) * 128],
                             i=g.ps.t[bank][:, 0:128]: E.tensor_copy(out=o, in_=i))
        em.mark("dve", "dvm")
        em.wait("pe", "dvm", em.cnt["dvm"])
        s2re4 = s2re.rearrange("p (g c b) -> p g c b", c=4, b=32)
        s2im4 = s2im.rearrange("p (g c b) -> p g c b", c=4, b=32)
        for beta in range(32):
            bank = g.ps.acquire("pe")
            em.step("pe", lambda E, o=g.ps.t[bank][:, 0:64], l=s2re4[:, :, :, beta], r=bcos:
                    E.matmul(o, l, r, start=True, stop=False))
            mv = em.step("pe", lambda E, o=g.ps.t[bank][:, 0:64], l=s2im4[:, :, :, beta], r=bnsin:
                         E.matmul(o, l, r, start=False, stop=True), inc="peq")
            em.wait("dve", "peq", mv)
            dst = ybuf.rearrange("p (a b) -> p b a", b=32)[:, beta]
            g.ps.release(bank, "dve", lambda E, o=dst, i=g.ps.t[bank][:, 0:64]:
                         E.tensor_copy(out=o, in_=i))
        em.wait("dve", "spq", qv)
        gtmp = temp[:, 0:2048]
        em.step("dve", lambda E, o=gtmp, a=kvb, s=small_col(g, cb, dsk), b2=ybuf:
                E.scalar_tensor_tensor(o, a, s, b2, ALU.mult, ALU.add))
        em.step("dve", lambda E, o=kvb, a=gtmp, b=qcb: E.tensor_tensor(o, a, b, ALU.mult))
        em.mark("dve", "dvm")
        em.wait("act", "dvm", em.cnt["dvm"])
        em.dma("act", g.s_yg[cb], kvb, inc="gq")
        em.wait("sp", "gq", em.cnt["gq"])
        em.barrier()
    em.mark("dve", "dvm")
    g.ss_done = ("dvm", em.cnt["dvm"])


def p_ssm_out(g):
    em, sb, d = g.em, g.sb, g.d
    R = sb["xres"][:]
    wbuf = [pc(g, 0, 2048), pc(g, 2048, 2048)]
    rbuf = pc(g, 4096, 1024)
    obuf = pc(g, 5120, 2048)
    em.wait("sp", g.ss_done[0], g.ss_done[1])
    em.wait("pe", g.ss_done[0], g.ss_done[1])
    em.wait("dve", g.ss_done[0], g.ss_done[1])
    em.wait("sp", "gq", g.em.cnt["gq"])
    for cc in range(NE):
        em.dma("sp", R[:, cc * T:(cc + 1) * T], g.s_yg[cc], inc="spq")
    em.wait("pe", "spq", em.cnt["spq"])
    so_it = [0]
    for eo in range(NE):
        bi = eo % 2
        if eo >= 2:
            em.wait("sp", "so_pe", em.cnt["so_pe"])
        wv = em.dma("sp", wbuf[bi].rearrange("p (c w) -> p c w", w=128),
                    d["woT"][:, eo * 128:(eo + 1) * 128].rearrange("(c p) w -> p c w", p=128),
                    inc="spq")
        em.wait("pe", "spq", wv)
        banks = [g.ps.acquire("pe", bank=tt) for tt in range(4)]
        for cc in range(NE):
            for tt in range(4):
                em.step("pe", lambda E, o=g.ps.t[banks[tt]][:, :],
                        l=wbuf[bi][:, cc * 128:(cc + 1) * 128],
                        r=R[:, cc * 2048 + tt * 512:cc * 2048 + tt * 512 + 512],
                        s=(cc == 0), pp=(cc == NE - 1): E.matmul(o, l, r, start=s, stop=pp))
        em.mark("pe", "peq")
        em.mark("pe", "so_pe")
        em.wait("dve", "peq", em.cnt["peq"])
        for tt in range(4):
            slot = rbuf[:, (tt % 2) * 512:(tt % 2) * 512 + 512]
            if so_it[0] >= 2:
                em.wait("sp", "so_rese", em.cnt["so_rese"])
            so_it[0] += 1
            rv = em.dma("sp", slot, g.s_xres[eo, :, tt * 512:(tt + 1) * 512], inc="spq")
            em.wait("dve", "spq", rv)
            em.step("dve", lambda E, o=obuf[:, tt * 512:(tt + 1) * 512], i=g.ps.t[banks[tt]][:, :],
                    s=small_col(g, eo, 9), a=slot:
                    E.scalar_tensor_tensor(o, i, s, a, ALU.add, ALU.add))
            g.ps.release(banks[tt], "dve", lambda E: E.drain())
            em.mark("dve", "so_rese")
        em.mark("dve", "dvm")
        em.wait("act", "dvm", em.cnt["dvm"])
        em.dma("act", g.s_x2[eo], obuf, inc="gq")
        em.wait("dve", "gq", em.cnt["gq"])
        em.barrier()
    em.wait("sp", "gq", em.cnt["gq"])
    for ec in range(NE):
        em.dma("sp", R[:, ec * T:(ec + 1) * T], g.s_x2[ec], inc="spq")
    em.wait("dve", "spq", em.cnt["spq"])
    em.mark("dve", "dvm")
    g.ss_out_done = ("dvm", em.cnt["dvm"])


def finish(g):
    em, sb = g.em, g.sb
    for nm, ap in g.tap_t.items():
        if nm == "xres":
            if getattr(g, "dvm", None):
                em.wait("act", g.a3_done[0], g.a3_done[1])
            elif getattr(g, "dvm", None):
                em.wait("act", g.ln_done[0], g.ln_done[1])
            else:
                em.wait("act", "dvm", g.xres_ready)
            em.wait("sp", g.a3_done[0], g.a3_done[1]) if getattr(g, "a3_done", None) else (
                em.wait("sp", g.ln_done[0], g.ln_done[1]) if getattr(g, "ln_done", None) else
                em.wait("sp", "dvm", g.xres_ready))
            em.dma("sp", ap, sb["xres"][:].rearrange("p (c t) -> p c t", t=T), inc="gq", delay=True)
        elif nm == "lnrows":
            em.wait("act", g.ln_done[0], g.ln_done[1])
            em.dma("act", ap, g.sb["pool"][0:1, 6144:6144 + 2560], inc="gq")
        elif nm == "lnab":
            em.wait("act", g.ln_done[0], g.ln_done[1])
            em.dma("act", ap, g.sb["pool"][:, 0:4096], inc="gq")
        elif nm == "lntmp":
            em.wait("act", g.ln_done[0], g.ln_done[1])
            em.dma("act", ap, g.sb["pool"][:, 4096:6144], inc="gq")
        elif nm == "lnr2":
            em.wait("act", g.ln_done[0], g.ln_done[1])
            em.dma("act", ap, g.sb["pool"][0:1, 8704:8704 + 4096], inc="gq")
        elif nm == "sx2":
            em.wait("act", g.ss_out_done[0], g.ss_out_done[1])
            em.dma("act", ap, g.s_x2, inc="gq")
        elif nm == "sx":
            em.wait("act", g.ln_done[0], g.ln_done[1])
            em.dma("act", ap, g.s_xres, inc="gq")
        elif nm == "sh":
            em.wait("act", "gq", em.cnt["gq"])
            em.dma("act", ap, g.s_h, inc="gq")
        elif nm in ("kv", "qc"):
            em.wait("act", g.sp_done[0], g.sp_done[1])
            em.dma("act", ap, {"kv": g.s_kv, "qc": g.s_qc}[nm], inc="gq")
        elif nm == "yg":
            em.wait("act", g.ss_done[0], g.ss_done[1])
            em.dma("act", ap, g.s_yg, inc="gq")
        elif nm == "ks":
            em.wait("act", g.ss_done[0], g.ss_done[1])
            em.dma("act", ap, g.s_ks, inc="gq")
        elif nm in ("wg", "wu", "wd"):
            em.wait("act", g.fb_done[0], g.fb_done[1])
            em.dma("act", ap, {"wg": g.s_wg, "wu": g.s_wu, "wd": g.s_wd}[nm], inc="gq")
        elif nm == "dbg0":
            pass
        elif nm == "dbg3":
            em.wait("act", "dbg3w", 1)
            em.wait("act", g.a3_done[0], g.a3_done[1])
            em.dma("act", ap, g.sb["pool"][:, 14600:14858], inc="gq")
        elif nm == "dbg1":
            em.wait("act", "dbg1w", 1)
            em.wait("act", g.a3_done[0], g.a3_done[1])
            em.dma("act", ap, g.sb["pool"][:, 14860:15244], inc="gq")
        elif nm == "wt0":
            pass
        elif nm in ("q", "k", "vt"):
            em.wait("act", g.a2_done[0], g.a2_done[1])
            src = {"q": g.s_q, "k": g.s_k, "vt": g.s_vt}[nm]
            em.dma("act", ap, src, inc="gq")
    if "gq" in em.cnt:
        em.wait("act", "gq", em.cnt["gq"])
    if "outd" in em.cnt:
        em.wait("act", "outd", em.cnt["outd"])


def kernel(**inputs):
    in_maps = host_prep(inputs)
    nc, g = build_program()
    res = run_bass_kernel_spmd(nc, in_maps, core_ids=list(range(8)))
    out = np.stack([res.results[i]["out"] for i in range(8)], axis=0)
    return out
